# version 30 (final)
# baseline (speedup 1.0000x reference)
import numpy as np
from contextlib import ExitStack
import concourse.bass as bass
import concourse.mybir as mybir
from concourse import dve_ops
from concourse.bass_utils import run_bass_kernel_spmd

F32 = mybir.dt.float32
BF16 = mybir.dt.bfloat16
I32 = mybir.dt.int32
I8 = mybir.dt.int8
AF = mybir.ActivationFunctionType
ALU = mybir.AluOpType
AX = mybir.AxisListType.X

B, S, D = 8, 2048, 1024
H, HD = 16, 64
NC_ = 8
CH = D // 128
QT = S // 512
KT = S // 128
MAGIC = 12582912.0
NEG_SHIFT = -40.0


class Ctr:

    def __init__(self, sem, step=1):
        self.sem = sem
        self.n = 0
        self.step = step


class Sched:

    def __init__(self):
        self.q = {"sync": [], "vector": [], "scalar": [], "tensor": [], "gpsimd": []}

    def add(self, eng, fn):
        self.q[eng].append(fn)

    def step(self, eng, fn, ctr, k=None, selfwait=True):
        if selfwait and eng in ("vector", "scalar", "gpsimd") and ctr.step == 1:
            self.wait(eng, ctr, ctr.n)
        kk = ctr.step if k is None else k
        ctr.n += kk
        sem = ctr.sem
        self.q[eng].append(lambda e: fn(e).then_inc(sem, kk))
        return ctr.n

    def wait(self, eng, ctr, val):
        v = int(val)
        if v > 0:
            sem = ctr.sem
            self.q[eng].append(lambda e: e.wait_ge(sem, v))

    def dma(self, eng, fn, ctr):
        self.wait(eng, ctr, ctr.n)
        return self.step(eng, fn, ctr)


def build_nc(debug=False):
    nc = bass.Bass("TRN2", target_bir_lowering=False, debug=False)

    xT = nc.dram_tensor("xT", [D, S], F32, kind="ExternalInput").ap()
    wqT = nc.dram_tensor("wqT", [D, D], F32, kind="ExternalInput").ap()
    wkT = nc.dram_tensor("wkT", [D, D], F32, kind="ExternalInput").ap()
    wvT = nc.dram_tensor("wvT", [D, D], F32, kind="ExternalInput").ap()
    woT = nc.dram_tensor("woT", [D, D], F32, kind="ExternalInput").ap()
    wfT = nc.dram_tensor("wfT", [D, D], I8, kind="ExternalInput").ap()
    bq = nc.dram_tensor("bq", [D], F32, kind="ExternalInput").ap()
    bk = nc.dram_tensor("bk", [D], F32, kind="ExternalInput").ap()
    bv = nc.dram_tensor("bv", [D], F32, kind="ExternalInput").ap()
    bo = nc.dram_tensor("bo", [D], F32, kind="ExternalInput").ap()
    fb = nc.dram_tensor("fb", [D], F32, kind="ExternalInput").ap()
    ws = nc.dram_tensor("ws", [D], F32, kind="ExternalInput").ap()
    outT = nc.dram_tensor("outT", [D, S], F32, kind="ExternalOutput").ap()

    scr_r = nc.dram_tensor("scr_r", [H * QT, 512], F32, kind="Internal").ap()
    scr_rows = nc.dram_tensor("scr_rows", [2, 512], F32, kind="Internal").ap()
    scr_red = nc.dram_tensor("scr_red", [1, 128], F32, kind="Internal").ap()
    scr_c = nc.dram_tensor("scr_c", [1, 4], F32, kind="Internal").ap()
    cc_in = nc.dram_tensor("cc_in", [1, 1], F32, kind="Internal").ap()
    cc_out = nc.dram_tensor("cc_out", [1, 1], F32, kind="Internal").ap()
    cc_in2 = nc.dram_tensor("cc_in2", [1, 1], F32, kind="Internal").ap()
    cc_out2 = nc.dram_tensor("cc_out2", [1, 1], F32, kind="Internal").ap()

    dbg = {}
    if debug:
        for nm, shp, dt in (("dQ", [128, CH * S], BF16), ("dK", [128, CH * S], BF16),
                            ("dV", [128, KT * H * 65], BF16),
                            ("dU", [128, CH * S], BF16),
                            ("dX1", [128, CH * S], F32), ("dXQ", [128, CH * S], BF16),
                            ("dNC", [128, 1], F32), ("dXS", [128, 1], F32)):
            dbg[nm] = nc.dram_tensor(nm, shp, dt, kind="ExternalOutput").ap()

    root = ExitStack()
    ph_ab = ExitStack()
    ph_wo = ExitStack()
    ph_c = ExitStack()

    def sb(stack, name, shape, dtype, side=None):
        return stack.enter_context(nc.sbuf_tensor(name, shape, dtype, side=side))

    def psum(stack, name, shape, side=None):
        return stack.enter_context(nc.psum_tensor(name, shape, F32, side=side))

    sems = {nm: root.enter_context(nc.semaphore(nm))
            for nm in ("dS", "dG", "tP", "sP", "vP", "cC", "dX0", "dX1", "dSM",
                       "gC", "dW0", "dW1", "dW2", "dW3", "dW4", "dW5",
                       "dW6", "dF0", "dF1", "dX2", "dX3", "dX4")}
    dS = Ctr(sems["dS"], 16)
    dG = Ctr(sems["dG"], 16)
    dX = [Ctr(sems["dX%d" % i], 16) for i in range(5)]
    dW = [Ctr(sems["dW%d" % i], 16) for i in range(7)]
    dF = [Ctr(sems["dF0"], 16), Ctr(sems["dF1"], 16)]
    dSM = Ctr(sems["dSM"], 16)
    tP = Ctr(sems["tP"])
    sP = Ctr(sems["sP"])
    vP = Ctr(sems["vP"])
    cC = Ctr(sems["cC"])
    gC = Ctr(sems["gC"])

    wcur = sb(root, "wcur", [128, CH * D], BF16)
    sqb = sb(root, "sqb", [128, 2048], BF16)
    bqs = sb(root, "bqs", [128, CH], F32)
    bks = sb(root, "bks", [128, CH], F32)
    bos = sb(root, "bos", [128, CH], F32)
    fbs = sb(root, "fbs", [128, CH], F32)
    wss = sb(root, "wss", [128, CH], F32)
    scs = sb(root, "scs", [128, CH], F32)
    ones1 = sb(root, "ones1", [128, 1], BF16)
    ones64 = sb(root, "ones64", [1, 64], BF16)
    ones2 = sb(root, "ones2", [128, 2], BF16)
    negc = sb(root, "negc", [128, 1], F32)
    magicb = sb(root, "magicb", [128, 1], F32)
    gmaxb = sb(root, "gmaxb", [128, 1], F32)
    xsb = sb(root, "xsb", [128, 1], F32)
    invxs = sb(root, "invxs", [128, 1], F32)
    nrm = sb(root, "nrm", [2, 1], F32)
    nrm2 = sb(root, "nrm2", [2, 1], F32)
    nrow = sb(root, "nrow", [1, 128], F32)
    sc1 = sb(root, "sc1", [1, 4], F32)
    red32 = sb(root, "red32", [128, QT * CH], F32)
    wstg4 = sb(root, "wstg4", [128, 512], F32)
    xstg2 = sb(root, "xstg2", [128, 512], F32)

    actbf = sb(root, "actbf", [128, CH * S], BF16)
    xbf = actbf
    ubf = actbf

    wnxt = sb(ph_wo, "wnxt", [128, CH * D], BF16, side="right")
    qbf = sb(ph_ab, "qbf", [128, CH * S], BF16, side="right")
    kbf = sb(ph_ab, "kbf", [128, CH * S], BF16, side="right")
    vpad = sb(ph_ab, "vpad", [128, KT * H * 65], BF16, side="right")
    rrow0 = sb(ph_ab, "rrow0", [1, 512], F32, side="right")
    rrow1 = sb(ph_ab, "rrow1", [1, 512], F32, side="right")
    rrow2 = sb(ph_ab, "rrow2", [1, 512], F32, side="right")
    rrow3 = sb(ph_ab, "rrow3", [1, 512], F32, side="right")
    rbt0 = sb(ph_ab, "rbt0", [64, 512], F32, side="right")
    rbt1 = sb(ph_ab, "rbt1", [64, 512], F32, side="right")
    rbt2 = sb(ph_ab, "rbt2", [64, 512], F32, side="right")
    rbt3 = sb(ph_ab, "rbt3", [64, 512], F32, side="right")
    unst = sb(ph_ab, "unst", [128, 512], BF16, side="right")
    bvb = sb(ph_ab, "bvb", [128, D], F32, side="right")
    expt = sb(ph_ab, "expt", [128, 4 * 1024], BF16, side="right")
    ust0 = sb(ph_ab, "ust0", [128, 512], F32, side="right")
    ust1 = sb(ph_ab, "ust1", [128, 512], F32, side="right")
    ust2 = sb(ph_ab, "ust2", [128, 512], F32, side="right")
    ust3 = sb(ph_ab, "ust3", [128, 512], F32, side="right")

    ph_pj = ExitStack()
    psS0 = psum(ph_ab, "psS0", [128, 1024])
    psS1 = psum(ph_ab, "psS1", [128, 1024])
    pu0 = psum(ph_ab, "pu0", [128, 512])
    pu1 = psum(ph_ab, "pu1", [128, 512])
    pj0 = psum(ph_pj, "pj0", [128, 512], side="right")
    pj1 = psum(ph_pj, "pj1", [128, 512], side="right")

    exptf = expt.bitcast(F32)
    qbf_f = qbf.bitcast(F32)
    kbf_f = kbf.bitcast(F32)
    xslots = [ust0[:], ust1[:], xstg2[:],
              exptf[:, 0:512], exptf[:, 512:1024]]
    wslots = [ust2[:], ust3[:], wstg4[:],
              exptf[:, 1024:1536], exptf[:, 1536:2048]]
    for tl in (qbf_f, kbf_f):
        base = 7 * (S // 2)
        for i in range(min(2, (S // 2) // 512)):
            if len(wslots) < 7:
                wslots.append(tl[:, base + i * 512: base + (i + 1) * 512])
    NXS = len(xslots)
    NWS = len(wslots)

    sch = Sched()

    for tile, src in ((bqs, bq), (bks, bk), (bos, bo), (fbs, fb), (wss, ws)):
        ap_in = src.rearrange("(c p) -> p c", p=128)
        sch.step("scalar", lambda e, t=tile, a=ap_in: e.dma_start(
            out=t[:], in_=a, allow_slow_non_contiguous=True), dSM)
    sch.step("scalar", lambda e: e.dma_start(
        out=bvb[:],
        in_=bv.rearrange("(o d) -> o d", o=1).broadcast_to([128, D])), dSM)
    d_small = dSM.n

    xstg_free_ms = [0] * NXS
    xw_ms = []
    XP = 512
    NXP = S // XP

    def stage_x_window(hf):
        for cc in range(CH):
            j = hf * CH + cc
            sl = j % NXS
            sch.wait("sync", sP, xstg_free_ms[sl])
            sch.dma("sync", lambda e, cc=cc, hf=hf, sl=sl: e.dma_start(
                out=xslots[sl],
                in_=xT[cc * 128:(cc + 1) * 128, hf * XP:(hf + 1) * XP]),
                dX[sl])
            sch.wait("scalar", dX[sl], dX[sl].n)
            sch.step("scalar", lambda e, cc=cc, hf=hf, a=xslots[sl]: e.activation(
                out=xbf[:, cc * S + hf * XP: cc * S + (hf + 1) * XP],
                in_=a, func=AF.Identity), sP)
            xstg_free_ms[sl] = sP.n
        xw_ms.append(sP.n)

    stage_x_window(0)

    sch.wait("vector", dSM, d_small)
    sch.step("vector", lambda e: e.tensor_scalar(
        out=bqs[:], in0=bqs[:], scalar1=0.125, scalar2=None, op0=ALU.mult), vP)
    sch.step("vector", lambda e: e.memset(ones1[:], 1.0), vP)
    sch.step("vector", lambda e: e.memset(ones64[:], 1.0), vP)
    sch.step("vector", lambda e: e.memset(ones2[0:64, 0:1], 1.0), vP)
    sch.step("vector", lambda e: e.memset(ones2[64:128, 0:1], 0.0), vP)
    sch.step("vector", lambda e: e.memset(ones2[0:64, 1:2], 0.0), vP)
    sch.step("vector", lambda e: e.memset(ones2[64:128, 1:2], 1.0), vP)
    vpad4 = vpad[:].rearrange("p (s k d) -> p s k d", s=8, k=2 * KT, d=65)
    sch.step("vector", lambda e: e.memset(vpad4[:, :, :, 64:65], 1.0), vP)
    sch.step("vector", lambda e: e.memset(nrm[:], 0.0), vP)
    prep_ms = vP.n

    wstg_free_ms = [0] * NWS

    def load_weight(src_ap, wbuf, stg=None, W=512, tguard=0, sync_v=0,
                    sync_t=0, qeng="gpsimd", chain=None, free_ms=None):
        if stg is None:
            stg = wslots
        if chain is None:
            chain = dW
        if free_ms is None:
            free_ms = wstg_free_ms
        nsl = len(stg)
        NH = D // W
        sch.wait("scalar", tP, tguard)
        if sync_v:
            sch.wait(qeng, vP, sync_v)
        if sync_t:
            sch.wait(qeng, tP, sync_t)
        for j in range(CH * NH):
            cc, h = j // NH, j % NH
            sl = j % nsl
            sch.wait(qeng, sP, free_ms[sl])
            sch.dma(qeng, lambda e, cc=cc, h=h, a=stg[sl]: e.dma_start(
                out=a,
                in_=src_ap[cc * 128:(cc + 1) * 128, h * W:(h + 1) * W]),
                chain[sl])
            sch.wait("scalar", chain[sl], chain[sl].n)
            sch.step("scalar", lambda e, cc=cc, h=h, a=stg[sl], w=wbuf:
                     e.activation(
                         out=w[:, cc * D + h * W: cc * D + (h + 1) * W],
                         in_=a, func=AF.Identity), sP)
            free_ms[sl] = sP.n
        return sP.n

    wv_ms = load_weight(wvT, wcur)
    for _hf in range(1, NXP):
        stage_x_window(_hf)

    pj = [pj0, pj1]
    pj_free_ms = [("v", 0), ("v", 0)]
    pj_idx = [0]

    def proj_group(wbuf, oc, mov_fn, wait_v=0, wait_s=0, wait_g=0, wait_g2=0):
        i = pj_idx[0]
        pj_idx[0] += 1
        slot = i % 2
        kind, ms = pj_free_ms[slot]
        sch.wait("tensor", vP if kind == "v" else sP, ms)
        if wait_v:
            sch.wait("tensor", vP, wait_v)
        if wait_s:
            sch.wait("tensor", sP, wait_s)
        if wait_g:
            sch.wait("tensor", dG, wait_g)
        if wait_g2:
            sch.wait("tensor", gC, wait_g2)
        for cc in range(CH):
            fn = lambda e, cc=cc, oc=oc, w=wbuf, bank=pj[slot]: e.matmul(
                bank[:, 0:512],
                w[:, cc * D + oc * 128: cc * D + (oc + 1) * 128],
                mov_fn(cc),
                start=(cc == 0), stop=(cc == CH - 1))
            if cc == CH - 1:
                sch.step("tensor", fn, tP)
            else:
                sch.add("tensor", fn)
        return slot, tP.n


    KC = 2 * KT
    for kap in range(KC):
        slot = kap % 2
        kind, ms = pj_free_ms[slot]
        sch.wait("tensor", vP if kind == "v" else sP, ms)
        sch.wait("tensor", sP, max(wv_ms, xw_ms[min(kap // 8, NXP - 1)]))
        for cc in range(CH):
            for dh in range(2):
                st = (cc == 0)
                last = (cc == CH - 1 and dh == 1)
                if dh == 0:
                    out_ap = pj[slot][0:64, 0:512]
                else:
                    out_ap = psS0[64:128, slot * 512:(slot + 1) * 512]
                fn = lambda e, cc=cc, dh=dh, kap=kap, st=st, o=out_ap: e.matmul(
                    o,
                    xbf[:, cc * S + kap * 64: cc * S + (kap + 1) * 64],
                    wcur[:, cc * D + dh * 512: cc * D + (dh + 1) * 512],
                    start=st, stop=(cc == CH - 1),
                    tile_position=(0, dh * 64), skip_group_check=True)
                if last:
                    sch.step("tensor", fn, tP)
                else:
                    sch.add("tensor", fn)
        sch.wait("vector", tP, tP.n)

        def ev(e, kap=kap, slot=slot, btop=pj[slot]):
            v4 = vpad[:].rearrange("p (s k d) -> p s k d", s=8, k=KC, d=65)
            out_top = v4[0:64, :, kap, 0:64]
            out_bot = v4[64:128, :, kap, 0:64]
            in_top = btop[0:64, 0:512].rearrange("p (s d) -> p s d", s=8, d=64)
            in_bot = psS0[64:128, slot * 512:(slot + 1) * 512].rearrange(
                "p (s d) -> p s d", s=8, d=64)
            bv_top = bvb[0:64, 0:512].rearrange("p (s d) -> p s d", s=8, d=64)
            bv_bot = bvb[64:128, 512:1024].rearrange("p (s d) -> p s d", s=8, d=64)
            e.tensor_tensor(out=out_top, in0=in_top, in1=bv_top, op=ALU.add)
            return e.tensor_tensor(out=out_bot, in0=in_bot, in1=bv_bot, op=ALU.add)
        sch.step("vector", ev, vP, k=1)
        pj_free_ms[slot] = ("v", vP.n)
    v_done_v = vP.n
    v_proj_t = tP.n


    wq_ms = load_weight(wqT, wcur, tguard=v_proj_t)
    wk_ms = load_weight(wkT, wnxt)

    sch.wait("scalar", vP, prep_ms)
    sch.wait("scalar", dSM, d_small)
    qk_chunk_s = []
    for oc in range(CH):
        for qt in range(QT):
            slot, ms = proj_group(
                wcur, oc,
                lambda cc, qt=qt: xbf[:, cc * S + qt * 512: cc * S + (qt + 1) * 512],
                wait_s=max(xw_ms[qt], wq_ms))
            sch.wait("scalar", tP, ms)
            sch.step("scalar", lambda e, oc=oc, qt=qt, b=pj[slot]: e.activation(
                out=qbf[:, oc * S + qt * 512: oc * S + (qt + 1) * 512],
                in_=b[:, 0:512], func=AF.Identity,
                bias=bqs[:, oc:oc + 1], scale=0.125), sP)
            pj_free_ms[slot] = ("s", sP.n)
        for qt in range(QT):
            slot, ms = proj_group(
                wnxt, oc,
                lambda cc, qt=qt: xbf[:, cc * S + qt * 512: cc * S + (qt + 1) * 512],
                wait_s=max(xw_ms[qt], wk_ms))
            sch.wait("scalar", tP, ms)
            sch.step("scalar", lambda e, oc=oc, qt=qt, b=pj[slot]: e.activation(
                out=kbf[:, oc * S + qt * 512: oc * S + (qt + 1) * 512],
                in_=b[:, 0:512], func=AF.Identity,
                bias=bks[:, oc:oc + 1], scale=1.0), sP)
            pj_free_ms[slot] = ("s", sP.n)
        qk_chunk_s.append(sP.n)
    q_done_s = sP.n
    k_done_s = sP.n
    q_proj_t = tP.n
    k_proj_t = tP.n

    wo_ms = load_weight(woT, wnxt, stg=wslots[:5], chain=dW[:5],
                        tguard=k_proj_t)
    ph_pj.close()
    pu2 = psum(ph_ab, "pu2", [128, 512], side="right")
    pu3 = psum(ph_ab, "pu3", [128, 512], side="right")

    sch.step("vector", lambda e: e.memset(negc[:], NEG_SHIFT), vP)
    sch.step("vector", lambda e: e.memset(magicb[:], float(MAGIC)), vP)
    negc_v_ms = vP.n

    psS = [psS0, psS1]
    psS_free_s = [0, 0]
    pu = [pu0, pu1, pu2, pu3]
    pu_free_v = [0, 0, 0, 0]
    ust = [ust0, ust1, ust2, ust3]
    ust_free = [0, 0, 0, 0]
    EXSL = 4
    ex_slot_free_t = [0] * EXSL
    rrow = [rrow0, rrow1, rrow2, rrow3]
    rbt = [rbt0, rbt1, rbt2, rbt3]
    rrow_free_g = [0] * 4
    rbt_free_v = [0] * 4
    unst_free_g = [0]
    first_exp = [True]
    exp_n = [0]
    sqb_guard_t = [0]
    uev = [0]

    def stage_scores_k(g, qt, kap):
        pg = kap % 2
        sch.wait("tensor", sP, psS_free_s[pg])
        if kap == 0:
            sch.wait("tensor", sP, qk_chunk_s[2 * g + 1])
        if first_exp[0] and kap == 0:
            sch.wait("tensor", vP, v_done_v)
        layout = (
            ((0, 0), 2 * g, 0, 0, 0),
            ((64, 64), 2 * g, 64, 0, 1),
            ((64, 0), 2 * g + 1, 64, 1, 0),
            ((0, 64), 2 * g + 1, 0, 1, 1),
        )
        for i, (tpos, ch, base, colh, rowh) in enumerate(layout):
            fn = lambda e, qt=qt, kap=kap, pg=pg, tpos=tpos, ch=ch, base=base, \
                colh=colh, rowh=rowh: e.matmul(
                psS[pg][rowh * 64: rowh * 64 + 64,
                        colh * 512:(colh + 1) * 512],
                kbf[base:base + 64,
                    ch * S + kap * 64: ch * S + (kap + 1) * 64],
                qbf[base:base + 64,
                    ch * S + qt * 512: ch * S + (qt + 1) * 512],
                start=True, stop=True, tile_position=tpos,
                skip_group_check=True)
            if i == 3:
                sch.step("tensor", fn, tP)
            else:
                sch.add("tensor", fn)
        sl = kap % EXSL
        if first_exp[0]:
            sch.wait("scalar", vP, negc_v_ms)
            first_exp[0] = False
        if exp_n[0] < EXSL:
            sch.wait("scalar", sP, wo_ms)
        exp_n[0] += 1
        sch.wait("scalar", tP, max(tP.n, ex_slot_free_t[sl]))
        sch.step("scalar", lambda e, sl=sl, pg=pg: e.activation(
            out=expt[:, sl * 1024:(sl + 1) * 1024],
            in_=psS[pg][:, 0:1024], func=AF.Exp,
            bias=negc[:], scale=1.0), sP, selfwait=False)
        psS_free_s[pg] = sP.n
        return sP.n

    def stage_av_k(g, kap, exp_ms):
        sl = kap % EXSL
        sch.wait("tensor", sP, exp_ms)
        work = (
            (0, 0, 2 * g, 0),
            (1, 64, 2 * g, 0),
            (2, 0, 2 * g + 1, 1),
            (3, 64, 2 * g + 1, 1),
        )
        for i, (bank, rb, scol, colh) in enumerate(work):
            fn = lambda e, kap=kap, sl=sl, bank=bank, rb=rb, scol=scol, \
                colh=colh: e.matmul(
                pu[bank][0:65, 0:512],
                vpad[rb:rb + 64,
                     scol * 65 * KC + kap * 65: scol * 65 * KC + (kap + 1) * 65],
                expt[rb:rb + 64,
                     sl * 1024 + colh * 512: sl * 1024 + (colh + 1) * 512],
                start=(kap == 0), stop=(kap == KC - 1),
                tile_position=(rb, 0), skip_group_check=True)
            if i == 3:
                sch.step("tensor", fn, tP)
            else:
                sch.add("tensor", fn)
        ex_slot_free_t[sl] = tP.n
        return tP.n

    def stage_u_copy(h, qt, bank, av_t):
        iu = uev[0] % 4
        uev[0] += 1
        sch.wait("vector", tP, av_t)
        if uev[0] <= 4:
            sch.wait("vector", sP, wo_ms)
        sch.wait("vector", vP, ust_free[iu])
        sch.wait("vector", dG, rrow_free_g[iu])
        sch.step("vector", lambda e, iu=iu, b=pu[bank]: e.tensor_copy(
            out=ust[iu][0:65, 0:512], in_=b[0:65, 0:512]), vP)
        pu_free_v[bank] = vP.n
        return (h, qt, iu)

    def stage_u_recip(info, fast=False):
        h, qt, iu = info
        if fast:
            sch.wait("scalar", vP, vP.n)
            sch.step("scalar", lambda e, iu=iu: e.activation(
                out=rrow[iu][0:1, 0:512], in_=ust[iu][64:65, 0:512],
                func=AF.Ln), sP)
            sch.step("scalar", lambda e, iu=iu: e.activation(
                out=sqb[0:1, iu * 512:(iu + 1) * 512], in_=rrow[iu][0:1, 0:512],
                func=AF.Exp, scale=-1.0), sP)
            sch.wait("tensor", sP, sP.n)
            sch.wait("tensor", vP, pu_free_v[iu])
            sch.step("tensor", lambda e, iu=iu: e.matmul(
                pu[iu][0:64, 0:512], ones64[0:1, 0:64],
                sqb[0:1, iu * 512:(iu + 1) * 512],
                start=True, stop=True), tP)
            sqb_guard_t[0] = tP.n
            return ("t", tP.n)
        sch.step("vector", lambda e, iu=iu: e.reciprocal(
            out=rrow[iu][0:1, 0:512], in_=ust[iu][64:65, 0:512]), vP)
        rms = vP.n
        sch.wait("gpsimd", vP, rms)
        sch.dma("gpsimd", lambda e, h=h, qt=qt, iu=iu: e.dma_start(
            out=scr_r[h * QT + qt: h * QT + qt + 1, :],
            in_=rrow[iu][0:1, 0:512]), dG)
        rrow_free_g[iu] = dG.n
        sch.wait("gpsimd", dG, dG.n)
        sch.wait("gpsimd", vP, rbt_free_v[iu])
        sch.dma("gpsimd", lambda e, h=h, qt=qt, iu=iu: e.dma_start(
            out=rbt[iu][0:64, 0:512],
            in_=scr_r[h * QT + qt: h * QT + qt + 1, :].broadcast_to([64, 512])),
            dG)
        return ("g", dG.n)

    def stage_u_norm(info, bc_ms):
        h, qt, iu = info
        ch = h // 2
        kind, ms = bc_ms
        if kind == "t":
            sch.wait("vector", tP, ms)
            rb = pu[iu]
        else:
            sch.wait("vector", dG, ms)
            rb = rbt[iu]
        if h % 2 == 0:
            sch.step("vector", lambda e, iu=iu, ch=ch, qt=qt, rb=rb:
                     e.tensor_tensor(
                         out=ubf[0:64, ch * S + qt * 512: ch * S + (qt + 1) * 512],
                         in0=ust[iu][0:64, 0:512], in1=rb[0:64, 0:512],
                         op=ALU.mult), vP)
        else:
            sch.wait("vector", dG, unst_free_g[0])
            sch.step("vector", lambda e, iu=iu, rb=rb: e.tensor_tensor(
                out=unst[0:64, 0:512],
                in0=ust[iu][0:64, 0:512], in1=rb[0:64, 0:512],
                op=ALU.mult), vP)
        rbt_free_v[iu] = vP.n
        ust_free[iu] = vP.n
        if h % 2 == 1:
            sch.wait("gpsimd", vP, vP.n)
            sch.dma("gpsimd", lambda e, ch=ch, qt=qt: e.dma_start(
                out=ubf[64:128, ch * S + qt * 512: ch * S + (qt + 1) * 512],
                in_=unst[0:64, 0:512]), dG)
            unst_free_g[0] = dG.n

    def stage_u_norms(infos, fast=False):
        bcs = [stage_u_recip(i, fast) for i in infos]
        for i, bc in zip(infos, bcs):
            stage_u_norm(i, bc)

    def stage_u_phase(g, qt, exp_last, fast=False):
        stage_av_k(g, KC - 1, exp_last)
        av_t = tP.n
        infos = [stage_u_copy(4 * g, qt, 0, av_t),
                 stage_u_copy(4 * g + 1, qt, 1, av_t),
                 stage_u_copy(4 * g + 3, qt, 2, av_t),
                 stage_u_copy(4 * g + 2, qt, 3, av_t)]
        stage_u_norms(infos, fast)

    sch.wait("gpsimd", vP, prep_ms)
    sch.dma("gpsimd", lambda e: e.dma_start(out=cc_in2, in_=nrm[0:1, 0:1]), dG)
    sch.wait("gpsimd", dG, dG.n)
    sch.step("gpsimd", lambda e: e.collective_compute(
        "AllReduce", ALU.max, replica_groups=[list(range(NC_))],
        ins=[cc_in2], outs=[cc_out2]), cC)

    u_qt_v = [0] * QT
    u_qt_g = [0] * QT
    prev = None
    for g in range(H // 4):
        for qt in range(QT):
            exp_ms = []
            for kap in range(KC):
                exp_ms.append(stage_scores_k(g, qt, kap))
                if kap == 0 and prev is not None:
                    stage_u_phase(*prev)
                    if prev[0] == H // 4 - 1:
                        u_qt_v[prev[1]] = vP.n
                        u_qt_g[prev[1]] = dG.n
                if kap == 2:
                    sch.wait("tensor", vP, max(pu_free_v))
                    stage_av_k(g, 0, exp_ms[0])
                    stage_av_k(g, 1, exp_ms[1])
                elif kap >= 3:
                    stage_av_k(g, kap - 1, exp_ms[kap - 1])
            prev = (g, qt, exp_ms[-1])
    stage_u_phase(*prev, fast=True)
    u_qt_v[prev[1]] = vP.n
    u_qt_g[prev[1]] = dG.n
    u_done_v = vP.n
    u_done_g = dG.n

    ph_ab.close()

    x1 = sb(root, "x1", [128, CH * S], F32)
    rowst = sb(root, "rowst", [1, 1024], F32)
    rtmp = sb(root, "rtmp", [1, 512], F32)
    rscr = sb(root, "rscr", [1, 512], F32)
    ztmp0 = sb(root, "ztmp0", [128, 512], F32)
    ztmp1 = sb(root, "ztmp1", [128, 512], F32)
    ztmp = [ztmp0, ztmp1]
    ztmp_free_v = [0, 0]
    mbeb = sb(root, "mbeb", [128, 1024], F32)
    xres0 = sb(ph_c, "xres0", [128, 512], F32)
    xres1 = sb(ph_c, "xres1", [128, 512], F32)
    xresg = [xres0, xres1]
    xres_free = [u_done_v, u_done_v]
    sch.wait("sync", dG, u_done_g)
    pjC0 = psum(root, "pjC0", [128, 512])
    pjC1 = psum(root, "pjC1", [128, 512])
    pj[0] = pjC0
    pj[1] = pjC1
    pj_free_ms[0] = ("v", 0)
    pj_free_ms[1] = ("v", 0)
    paA = psum(root, "paA", [128, 512])
    paB = psum(root, "paB", [128, 512])

    ri = [0]

    def stage_o_proj(qt):
        for cc in range(CH):
            k = ri[0] % 2
            ri[0] += 1
            sch.wait("sync", vP, xres_free[k])
            sch.dma("sync", lambda e, cc=cc, qt=qt, k=k: e.dma_start(
                out=xresg[k][:],
                in_=xT[cc * 128:(cc + 1) * 128, qt * 512:(qt + 1) * 512]), dX[k])
            x_rdy = dX[k].n
            slot, ms = proj_group(
                wnxt, cc,
                lambda ic, qt=qt: ubf[:, ic * S + qt * 512: ic * S + (qt + 1) * 512],
                wait_v=u_qt_v[qt], wait_s=wo_ms, wait_g=u_qt_g[qt])
            sch.wait("scalar", tP, ms)
            zi = pj_idx[0] % 2
            zt = ztmp[zi]
            sch.wait("scalar", vP, ztmp_free_v[zi])
            sch.step("scalar", lambda e, cc=cc, b=pj[slot], zt=zt: e.activation(
                out=zt[:], in_=b[:, 0:512], func=AF.Identity,
                bias=bos[:, cc:cc + 1]), sP)
            pj_free_ms[slot] = ("s", sP.n)
            sch.wait("vector", sP, sP.n)
            sch.wait("vector", dX[k], x_rdy)
            sch.step("vector", lambda e, cc=cc, qt=qt, zt=zt, k=k:
                     e.tensor_tensor(
                         out=x1[:, cc * S + qt * 512: cc * S + (qt + 1) * 512],
                         in0=zt[:], in1=xresg[k][:], op=ALU.add), vP)
            ztmp_free_v[zi] = vP.n
            xres_free[k] = vP.n
        return vP.n

    rowst_free_g = [0]

    def stage_ln_qt(buf, qt, ready_v, stream_out=False):
        sch.wait("scalar", vP, ready_v)
        sch.wait("scalar", tP, sqb_guard_t[0])
        stat_t = []
        for cc in range(CH):
            sl0 = cc * S + qt * 512
            sb0 = (cc % 2) * 1024
            if cc >= 2:
                sch.wait("scalar", tP, stat_t[cc - 2])
            sch.step("scalar", lambda e, sl0=sl0, sb0=sb0: e.activation(
                out=sqb[:, sb0:sb0 + 512], in_=buf[:, sl0:sl0 + 512],
                func=AF.Identity), sP)
            sch.step("scalar", lambda e, sl0=sl0, sb0=sb0: e.activation(
                out=sqb[:, sb0 + 512:sb0 + 1024], in_=buf[:, sl0:sl0 + 512],
                func=AF.Square), sP)
            sch.wait("tensor", sP, sP.n)
            for j in range(2):
                bank = paA if j == 0 else paB
                fn = lambda e, j=j, sb0=sb0, bank=bank, st=(cc == 0), \
                    sp=(cc == CH - 1): e.matmul(
                    bank[0:1, 0:512], ones1[:],
                    sqb[:, sb0 + j * 512: sb0 + (j + 1) * 512],
                    start=st, stop=sp)
                if j == 1:
                    sch.step("tensor", fn, tP)
                else:
                    sch.add("tensor", fn)
            stat_t.append(tP.n)
        sch.wait("vector", tP, tP.n)
        sch.wait("vector", dG, rowst_free_g[0])
        sch.step("vector", lambda e: e.tensor_scalar(
            out=rowst[0:1, 0:512], in0=paA[0:1, 0:512],
            scalar1=float(1.0 / D), scalar2=None, op0=ALU.mult), vP)
        sch.step("vector", lambda e: e.tensor_scalar(
            out=rowst[0:1, 512:1024], in0=paB[0:1, 0:512],
            scalar1=float(1.0 / D), scalar2=None, op0=ALU.mult), vP)
        sch.step("vector", lambda e: e.tensor_tensor(
            out=rtmp[0:1, :], in0=rowst[0:1, 0:512], in1=rowst[0:1, 0:512],
            op=ALU.mult), vP)
        sch.step("vector", lambda e: e.scalar_tensor_tensor(
            out=rowst[0:1, 512:1024], in0=rowst[0:1, 512:1024], scalar=1e-5,
            in1=rtmp[0:1, :], op0=ALU.add, op1=ALU.subtract), vP)
        sch.wait("scalar", vP, vP.n)
        sch.wait("scalar", dG, rowst_free_g[0])
        sch.step("scalar", lambda e: e.activation(
            out=rowst[0:1, 512:1024], in_=rowst[0:1, 512:1024],
            func=AF.Ln), sP)
        sch.step("scalar", lambda e: e.activation(
            out=rowst[0:1, 512:1024], in_=rowst[0:1, 512:1024],
            func=AF.Exp, scale=-0.5), sP)
        sch.wait("vector", sP, sP.n)
        sch.wait("gpsimd", vP, vP.n)
        sch.wait("gpsimd", sP, sP.n)
        sch.dma("gpsimd", lambda e: e.dma_start(
            out=scr_rows, in_=rowst[0:1, :]), dG)
        rowst_free_g[0] = dG.n
        sch.wait("gpsimd", dG, dG.n)
        sch.dma("gpsimd", lambda e: e.dma_start(
            out=mbeb[:],
            in_=scr_rows.rearrange("a b -> (a b)")
            .rearrange("(o f) -> o f", o=1).broadcast_to([128, 1024])), dG)
        sch.wait("vector", dG, dG.n)
        sch.wait("gpsimd", dG, dG.n)
        if not stream_out:
            ba = buf[:].rearrange("p (c s) -> p c s", c=CH, s=S)[
                :, :, qt * 512:(qt + 1) * 512]
            mb_s = mbeb[:, 0:512].unsqueeze(1).broadcast_to([128, CH, 512])
            rb_s = mbeb[:, 512:1024].unsqueeze(1).broadcast_to([128, CH, 512])
            sch.step("vector", lambda e, ba=ba, m=mb_s: e.tensor_tensor(
                out=ba, in0=ba, in1=m, op=ALU.subtract), vP)
            sch.step("vector", lambda e, ba=ba, m=rb_s: e.tensor_tensor(
                out=ba, in0=ba, in1=m, op=ALU.mult), vP)
            return (vP.n, gC.n)
        sch.wait("sync", gC, gC.n)
        sch.wait("gpsimd", gC, gC.n)
        for cc in range(CH):
            sl0 = cc * S + qt * 512
            sch.step("vector", lambda e, sl0=sl0: e.tensor_tensor(
                out=buf[:, sl0:sl0 + 512], in0=buf[:, sl0:sl0 + 512],
                in1=mbeb[:, 0:512], op=ALU.subtract), vP)
            sch.step("vector", lambda e, sl0=sl0: e.tensor_tensor(
                out=buf[:, sl0:sl0 + 512], in0=buf[:, sl0:sl0 + 512],
                in1=mbeb[:, 512:1024], op=ALU.mult), vP)
            eng, ctr = (("sync", dS) if cc % 2 == 0 else ("gpsimd", dG))
            sch.wait(eng, vP, vP.n)
            sch.dma(eng, lambda e, cc=cc, sl0=sl0, qt=qt: e.dma_start(
                out=outT[cc * 128:(cc + 1) * 128, qt * 512:(qt + 1) * 512],
                in_=buf[:, sl0:sl0 + 512]), ctr)
        return (vP.n, gC.n)

    def stage_absmax_qt(qt):
        for cc in range(CH):
            sch.step("vector", lambda e, cc=cc, qt=qt: e.tensor_reduce(
                out=red32[:, qt * CH + cc: qt * CH + cc + 1],
                in_=x1[:, cc * S + qt * 512: cc * S + (qt + 1) * 512], axis=AX,
                op=ALU.max, apply_absolute_value=True), vP)

    oq_ms = []
    for qt in range(QT):
        oq_ms.append(stage_o_proj(qt))
        if qt >= 1:
            stage_ln_qt(x1, qt - 1, oq_ms[qt - 1])
            stage_absmax_qt(qt - 1)
    o_proj_t = tP.n
    xres_done_v = vP.n
    ph_c.close()
    wstgi0 = sb(root, "wstgi0", [128, D], I8)
    wstgi1 = sb(root, "wstgi1", [128, D], I8)
    ln1_v, ln1_g = stage_ln_qt(x1, QT - 1, oq_ms[QT - 1])
    stage_absmax_qt(QT - 1)
    wf_ms = load_weight(wfT, wcur, [wstgi0[:], wstgi1[:]], W=1024,
                        tguard=q_proj_t, sync_v=xres_done_v, sync_t=o_proj_t,
                        qeng="sync", chain=dF, free_ms=[0, 0])
    sch.wait("vector", gC, ln1_g)
    ln1_ms = vP.n

    sch.step("vector", lambda e: e.tensor_reduce(
        out=gmaxb[:], in_=red32[:], axis=AX, op=ALU.max,
        apply_absolute_value=False), vP)
    sch.wait("gpsimd", vP, vP.n)
    sch.dma("gpsimd", lambda e: e.dma_start(out=scr_red, in_=gmaxb[:]), dG)
    sch.wait("gpsimd", dG, dG.n)
    sch.dma("gpsimd", lambda e: e.dma_start(out=nrow[:], in_=scr_red), dG)
    sch.wait("vector", dG, dG.n)
    sch.step("vector", lambda e: e.tensor_reduce(
        out=sc1[0:1, 0:1], in_=nrow[:], axis=AX, op=ALU.max,
        apply_absolute_value=False), vP)
    sch.wait("gpsimd", vP, vP.n)
    sch.dma("gpsimd", lambda e: e.dma_start(out=cc_in, in_=sc1[0:1, 0:1]), dG)
    sch.wait("gpsimd", dG, dG.n)
    sch.step("gpsimd", lambda e: e.collective_compute(
        "AllReduce", ALU.max, replica_groups=[list(range(NC_))],
        ins=[cc_in], outs=[cc_out]), cC)
    sch.wait("gpsimd", cC, 2)
    sch.dma("gpsimd", lambda e: e.dma_start(
        out=gmaxb[:], in_=cc_out.broadcast_to([128, 1])), dG)
    sch.wait("vector", dG, dG.n)
    sch.step("vector", lambda e: e.tensor_scalar(
        out=xsb[:], in0=gmaxb[:], scalar1=float(1.0 / 127.0), scalar2=1e-5,
        op0=ALU.mult, op1=ALU.max), vP)
    sch.step("vector", lambda e: e.reciprocal(out=invxs[:], in_=xsb[:]), vP)
    sch.step("vector", lambda e: e.tensor_tensor(
        out=scs[:], in0=wss[:], in1=xsb[:].broadcast_to([128, CH]),
        op=ALU.mult), vP)
    xs_ms = vP.n

    ph_wo.close()
    xq = actbf
    y2 = sb(root, "y2", [128, CH * S], F32)
    q32a = sb(root, "q32a", [128, 512], F32)
    q32a2 = sb(root, "q32a2", [128, 512], F32)
    q32ab = [q32a, q32a2]
    q32_free = [0, 0]

    def stage_quant_cc(qt, cc):
        sl0 = cc * S + qt * 512
        if True:
            sch.wait("scalar", vP, q32_free[cc % 2])
            sch.step("scalar", lambda e, sl0=sl0, cc=cc: e.activation(
                out=q32ab[cc % 2][:], in_=x1[:, sl0:sl0 + 512],
                func=AF.Identity, scale=invxs[:],
                bias=magicb[:]), sP)
            ms1 = sP.n
            sch.wait("vector", sP, ms1)
            sch.step("vector", lambda e, cc=cc: e.tensor_scalar(
                out=q32ab[cc % 2][:], in0=q32ab[cc % 2][:], scalar1=float(MAGIC),
                scalar2=127.0, op0=ALU.subtract, op1=ALU.min), vP)
            sch.step("vector", lambda e, cc=cc, sl0=sl0: e.tensor_scalar(
                out=xq[:, sl0:sl0 + 512], in0=q32ab[cc % 2][:], scalar1=-128.0,
                scalar2=None, op0=ALU.max), vP)
            q32_free[cc % 2] = vP.n
            sch.step("scalar", lambda e, cc=cc, sl0=sl0: e.activation(
                out=x1[:, sl0:sl0 + 512], in_=x1[:, sl0:sl0 + 512],
                func=AF.Identity, scale=1.0, bias=fbs[:, cc:cc + 1]), sP)

    def stage_quant(qt):
        sch.wait("vector", vP, xs_ms)
        sch.wait("vector", tP, o_proj_t)
        sch.wait("scalar", vP, xs_ms)
        sch.wait("scalar", tP, o_proj_t)
        for cc in range(CH):
            stage_quant_cc(qt, cc)
        return (vP.n, gC.n)

    def stage_ffn(qt, q_ms):
        for cc in range(CH):
            slot, ms = proj_group(
                wcur, cc,
                lambda ic, qt=qt: xq[:, ic * S + qt * 512: ic * S + (qt + 1) * 512],
                wait_v=q_ms[0], wait_s=wf_ms, wait_g2=q_ms[1])
            sch.wait("scalar", tP, ms)
            zi = pj_idx[0] % 2
            zt = ztmp[zi]
            sch.wait("scalar", vP, ztmp_free_v[zi])
            sch.step("scalar", lambda e, cc=cc, b=pj[slot], zt=zt: e.activation(
                out=zt[:], in_=b[:, 0:512], func=AF.Identity,
                scale=scs[:, cc:cc + 1]), sP)
            pj_free_ms[slot] = ("s", sP.n)
            sch.wait("vector", sP, sP.n)
            sch.step("vector", lambda e, cc=cc, qt=qt, zt=zt:
                     e.tensor_tensor(
                         out=y2[:, cc * S + qt * 512: cc * S + (qt + 1) * 512],
                         in0=zt[:],
                         in1=x1[:, cc * S + qt * 512: cc * S + (qt + 1) * 512],
                         op=ALU.add), vP)
            ztmp_free_v[zi] = vP.n
            if qt + 1 < QT:
                stage_quant_cc(qt + 1, cc)
        return vP.n, (vP.n, gC.n)

    ln2_qt_ms = [0] * QT
    ffn_ms = [0] * QT
    q_next = stage_quant(0)
    for qt in range(QT):
        ffn_ms[qt], q_next = stage_ffn(qt, q_next)
        if qt >= 1:
            ln2_qt_ms[qt - 1] = stage_ln_qt(y2, qt - 1, ffn_ms[qt - 1])
            for cc in range(CH):
                sch.wait("sync", vP, ln2_qt_ms[qt - 1][0])
                sch.wait("sync", gC, ln2_qt_ms[qt - 1][1])
                sch.dma("sync", lambda e, cc=cc, qt=qt: e.dma_start(
                    out=outT[cc * 128:(cc + 1) * 128,
                             (qt - 1) * 512: qt * 512],
                    in_=y2[:, cc * S + (qt - 1) * 512: cc * S + qt * 512]), dS)
    ln2_qt_ms[QT - 1] = stage_ln_qt(y2, QT - 1, ffn_ms[QT - 1],
                                    stream_out=True)

    if debug:
        sch.wait("sync", vP, vP.n)
        sch.wait("sync", sP, sP.n)
        sch.wait("sync", dG, dG.n)
        for nm, buf in (("dX1", x1), ("dXQ", xq), ("dXS", xsb)):
            sch.dma("sync", lambda e, nm=nm, b=buf: e.dma_start(
                out=dbg[nm], in_=b[:]), dS)

    sch.wait("sync", dS, dS.n)
    sch.wait("sync", dG, dG.n)

    with nc.Block() as block:
        @block.sync
        def _(sync):
            for fn in sch.q["sync"]:
                fn(sync)

        @block.vector
        def _(vector):
            for fn in sch.q["vector"]:
                fn(vector)

        @block.scalar
        def _(scalar):
            for fn in sch.q["scalar"]:
                fn(scalar)

        @block.tensor
        def _(tensor):
            for fn in sch.q["tensor"]:
                fn(tensor)

        @block.gpsimd
        def _(gpsimd):
            for fn in sch.q["gpsimd"]:
                fn(gpsimd)

    root.close()
    return nc


_NC_CACHE = {}


def _get_nc(debug=False):
    if debug not in _NC_CACHE:
        _NC_CACHE[debug] = build_nc(debug)
    return _NC_CACHE[debug]


def make_in_maps(L0_attn_x, L0_attn_Wq, L0_attn_bq, L0_attn_Wk, L0_attn_bk,
                 L0_attn_Wv, L0_attn_bv, L0_attn_Wo, L0_attn_bo,
                 L0_ffn_weight_int8, L0_ffn_weight_scale, L0_ffn_bias):
    x = np.asarray(L0_attn_x, dtype=np.float32)
    shared = {
        "wqT": np.ascontiguousarray(np.asarray(L0_attn_Wq, np.float32).T),
        "wkT": np.ascontiguousarray(np.asarray(L0_attn_Wk, np.float32).T),
        "wvT": np.ascontiguousarray(
            np.asarray(L0_attn_Wv, np.float32).T.reshape(D, 16, 64)
            [:, [0, 3, 4, 7, 8, 11, 12, 15, 1, 2, 5, 6, 9, 10, 13, 14], :]
            .reshape(D, D)),
        "woT": np.ascontiguousarray(np.asarray(L0_attn_Wo, np.float32).T),
        "wfT": np.ascontiguousarray(np.asarray(L0_ffn_weight_int8, np.int8).T),
        "bq": np.ascontiguousarray(np.asarray(L0_attn_bq, np.float32)),
        "bk": np.ascontiguousarray(np.asarray(L0_attn_bk, np.float32)),
        "bv": np.ascontiguousarray(
            np.asarray(L0_attn_bv, np.float32).reshape(16, 64)
            [[0, 3, 4, 7, 8, 11, 12, 15, 1, 2, 5, 6, 9, 10, 13, 14], :]
            .reshape(D)),
        "bo": np.ascontiguousarray(np.asarray(L0_attn_bo, np.float32)),
        "fb": np.ascontiguousarray(np.asarray(L0_ffn_bias, np.float32)),
        "ws": np.ascontiguousarray(
            np.asarray(L0_ffn_weight_scale, np.float32).reshape(-1)),
    }
    in_maps = []
    for b in range(B):
        m = dict(shared)
        m["xT"] = np.ascontiguousarray(x[b].T)
        in_maps.append(m)
    return in_maps


def kernel(**inputs):
    nc = _get_nc(debug=False)
    in_maps = make_in_maps(**inputs)
    res = run_bass_kernel_spmd(nc, in_maps, core_ids=list(range(NC_)))
    out = np.empty((B, S, D), dtype=np.float32)
    for b in range(B):
        out[b] = res.results[b]["outT"].T
    return out


# version 31
# speedup vs baseline: 1.0182x; 1.0182x over previous
import numpy as np
from contextlib import ExitStack
import concourse.bass as bass
import concourse.mybir as mybir
from concourse import dve_ops
from concourse.bass_utils import run_bass_kernel_spmd

F32 = mybir.dt.float32
BF16 = mybir.dt.bfloat16
I32 = mybir.dt.int32
I8 = mybir.dt.int8
AF = mybir.ActivationFunctionType
ALU = mybir.AluOpType
AX = mybir.AxisListType.X

B, S, D = 8, 2048, 1024
H, HD = 16, 64
NC_ = 8
CH = D // 128
QT = S // 512
KT = S // 128
MAGIC = 12582912.0
NEG_SHIFT = -40.0


class Ctr:

    def __init__(self, sem, step=1):
        self.sem = sem
        self.n = 0
        self.step = step


class Sched:

    def __init__(self):
        self.q = {"sync": [], "vector": [], "scalar": [], "tensor": [], "gpsimd": []}

    def add(self, eng, fn):
        self.q[eng].append(fn)

    def step(self, eng, fn, ctr, k=None, selfwait=True):
        if selfwait and eng in ("vector", "scalar", "gpsimd") and ctr.step == 1:
            self.wait(eng, ctr, ctr.n)
        kk = ctr.step if k is None else k
        ctr.n += kk
        sem = ctr.sem
        self.q[eng].append(lambda e: fn(e).then_inc(sem, kk))
        return ctr.n

    def wait(self, eng, ctr, val):
        v = int(val)
        if v > 0:
            sem = ctr.sem
            self.q[eng].append(lambda e: e.wait_ge(sem, v))

    def dma(self, eng, fn, ctr):
        self.wait(eng, ctr, ctr.n)
        return self.step(eng, fn, ctr)


def build_nc(debug=False):
    nc = bass.Bass("TRN2", target_bir_lowering=False, debug=False)

    xT = nc.dram_tensor("xT", [D, S], F32, kind="ExternalInput").ap()
    wqT = nc.dram_tensor("wqT", [D, D], F32, kind="ExternalInput").ap()
    wkT = nc.dram_tensor("wkT", [D, D], F32, kind="ExternalInput").ap()
    wvT = nc.dram_tensor("wvT", [D, D], F32, kind="ExternalInput").ap()
    woT = nc.dram_tensor("woT", [D, D], F32, kind="ExternalInput").ap()
    wfT = nc.dram_tensor("wfT", [D, D], I8, kind="ExternalInput").ap()
    bq = nc.dram_tensor("bq", [D], F32, kind="ExternalInput").ap()
    bk = nc.dram_tensor("bk", [D], F32, kind="ExternalInput").ap()
    bv = nc.dram_tensor("bv", [D], F32, kind="ExternalInput").ap()
    bo = nc.dram_tensor("bo", [D], F32, kind="ExternalInput").ap()
    fb = nc.dram_tensor("fb", [D], F32, kind="ExternalInput").ap()
    ws = nc.dram_tensor("ws", [D], F32, kind="ExternalInput").ap()
    outT = nc.dram_tensor("outT", [D, S], F32, kind="ExternalOutput").ap()

    scr_r = nc.dram_tensor("scr_r", [H * QT, 512], F32, kind="Internal").ap()
    scr_rows = nc.dram_tensor("scr_rows", [2, 512], F32, kind="Internal").ap()
    scr_red = nc.dram_tensor("scr_red", [1, 128], F32, kind="Internal").ap()
    scr_c = nc.dram_tensor("scr_c", [1, 4], F32, kind="Internal").ap()
    cc_in = nc.dram_tensor("cc_in", [1, 1], F32, kind="Internal").ap()
    cc_out = nc.dram_tensor("cc_out", [1, 1], F32, kind="Internal").ap()
    cc_in2 = nc.dram_tensor("cc_in2", [1, 1], F32, kind="Internal").ap()
    cc_out2 = nc.dram_tensor("cc_out2", [1, 1], F32, kind="Internal").ap()

    dbg = {}
    if debug:
        for nm, shp, dt in (("dQ", [128, CH * S], BF16), ("dK", [128, CH * S], BF16),
                            ("dV", [128, KT * H * 65], BF16),
                            ("dU", [128, CH * S], BF16),
                            ("dX1", [128, CH * S], F32), ("dXQ", [128, CH * S], BF16),
                            ("dNC", [128, 1], F32), ("dXS", [128, 1], F32)):
            dbg[nm] = nc.dram_tensor(nm, shp, dt, kind="ExternalOutput").ap()

    root = ExitStack()
    ph_ab = ExitStack()
    ph_wo = ExitStack()
    ph_c = ExitStack()

    def sb(stack, name, shape, dtype, side=None):
        return stack.enter_context(nc.sbuf_tensor(name, shape, dtype, side=side))

    def psum(stack, name, shape, side=None):
        return stack.enter_context(nc.psum_tensor(name, shape, F32, side=side))

    sems = {nm: root.enter_context(nc.semaphore(nm))
            for nm in ("dS", "dG", "tP", "sP", "vP", "cC", "dX0", "dX1", "dSM",
                       "gC", "dW0", "dW1", "dW2", "dW3", "dW4", "dW5",
                       "dW6", "dF0", "dF1", "dX2", "dX3", "dX4")}
    dS = Ctr(sems["dS"], 16)
    dG = Ctr(sems["dG"], 16)
    dX = [Ctr(sems["dX%d" % i], 16) for i in range(5)]
    dW = [Ctr(sems["dW%d" % i], 16) for i in range(7)]
    dF = [Ctr(sems["dF0"], 16), Ctr(sems["dF1"], 16)]
    dSM = Ctr(sems["dSM"], 16)
    tP = Ctr(sems["tP"])
    sP = Ctr(sems["sP"])
    vP = Ctr(sems["vP"])
    cC = Ctr(sems["cC"])
    gC = Ctr(sems["gC"])

    wcur = sb(root, "wcur", [128, CH * D], BF16)
    sqb = sb(root, "sqb", [128, 2048], BF16)
    bqs = sb(root, "bqs", [128, CH], F32)
    bks = sb(root, "bks", [128, CH], F32)
    bos = sb(root, "bos", [128, CH], F32)
    fbs = sb(root, "fbs", [128, CH], F32)
    wss = sb(root, "wss", [128, CH], F32)
    scs = sb(root, "scs", [128, CH], F32)
    ones1 = sb(root, "ones1", [128, 1], BF16)
    ones64 = sb(root, "ones64", [1, 64], BF16)
    ones2 = sb(root, "ones2", [128, 2], BF16)
    negc = sb(root, "negc", [128, 1], F32)
    magicb = sb(root, "magicb", [128, 1], F32)
    gmaxb = sb(root, "gmaxb", [128, 1], F32)
    xsb = sb(root, "xsb", [128, 1], F32)
    invxs = sb(root, "invxs", [128, 1], F32)
    nrm = sb(root, "nrm", [2, 1], F32)
    nrm2 = sb(root, "nrm2", [2, 1], F32)
    nrow = sb(root, "nrow", [1, 128], F32)
    sc1 = sb(root, "sc1", [1, 4], F32)
    red32 = sb(root, "red32", [128, QT * CH], F32)
    wstg4 = sb(root, "wstg4", [128, 512], F32)
    xstg2 = sb(root, "xstg2", [128, 512], F32)

    actbf = sb(root, "actbf", [128, CH * S], BF16)
    xbf = actbf
    ubf = actbf

    wnxt = sb(ph_wo, "wnxt", [128, CH * D], BF16, side="right")
    qbf = sb(ph_ab, "qbf", [128, CH * S], BF16, side="right")
    kbf = sb(ph_ab, "kbf", [128, CH * S], BF16, side="right")
    vpad = sb(ph_ab, "vpad", [128, KT * H * 65], BF16, side="right")
    rrow0 = sb(ph_ab, "rrow0", [1, 512], F32, side="right")
    rrow1 = sb(ph_ab, "rrow1", [1, 512], F32, side="right")
    rrow2 = sb(ph_ab, "rrow2", [1, 512], F32, side="right")
    rrow3 = sb(ph_ab, "rrow3", [1, 512], F32, side="right")
    rbt0 = sb(ph_ab, "rbt0", [64, 512], F32, side="right")
    rbt1 = sb(ph_ab, "rbt1", [64, 512], F32, side="right")
    rbt2 = sb(ph_ab, "rbt2", [64, 512], F32, side="right")
    rbt3 = sb(ph_ab, "rbt3", [64, 512], F32, side="right")
    unst = sb(ph_ab, "unst", [128, 512], BF16, side="right")
    bvb = sb(ph_ab, "bvb", [128, D], F32, side="right")
    expt = sb(ph_ab, "expt", [128, 4 * 1024], BF16, side="right")
    ust0 = sb(ph_ab, "ust0", [128, 512], F32, side="right")
    ust1 = sb(ph_ab, "ust1", [128, 512], F32, side="right")
    ust2 = sb(ph_ab, "ust2", [128, 512], F32, side="right")
    ust3 = sb(ph_ab, "ust3", [128, 512], F32, side="right")

    ph_pj = ExitStack()
    psS0 = psum(ph_ab, "psS0", [128, 1024])
    psS1 = psum(ph_ab, "psS1", [128, 1024])
    pu0 = psum(ph_ab, "pu0", [128, 512])
    pu1 = psum(ph_ab, "pu1", [128, 512])
    pj0 = psum(ph_pj, "pj0", [128, 512], side="right")
    pj1 = psum(ph_pj, "pj1", [128, 512], side="right")

    exptf = expt.bitcast(F32)
    qbf_f = qbf.bitcast(F32)
    kbf_f = kbf.bitcast(F32)
    xslots = [ust0[:], ust1[:], xstg2[:],
              exptf[:, 0:512], exptf[:, 512:1024]]
    wslots = [ust2[:], ust3[:], wstg4[:],
              exptf[:, 1024:1536], exptf[:, 1536:2048]]
    for tl in (qbf_f, kbf_f):
        base = 7 * (S // 2)
        for i in range(min(2, (S // 2) // 512)):
            if len(wslots) < 7:
                wslots.append(tl[:, base + i * 512: base + (i + 1) * 512])
    NXS = len(xslots)
    NWS = len(wslots)

    sch = Sched()

    for tile, src in ((bqs, bq), (bks, bk), (bos, bo), (fbs, fb), (wss, ws)):
        ap_in = src.rearrange("(c p) -> p c", p=128)
        sch.step("scalar", lambda e, t=tile, a=ap_in: e.dma_start(
            out=t[:], in_=a, allow_slow_non_contiguous=True), dSM)
    sch.step("scalar", lambda e: e.dma_start(
        out=bvb[:],
        in_=bv.rearrange("(o d) -> o d", o=1).broadcast_to([128, D])), dSM)
    d_small = dSM.n

    xstg_free_ms = [0] * NXS
    xw_ms = []
    XP = 512
    NXP = S // XP

    def stage_x_window(hf):
        for cc in range(CH):
            j = hf * CH + cc
            sl = j % NXS
            sch.wait("sync", sP, xstg_free_ms[sl])
            sch.dma("sync", lambda e, cc=cc, hf=hf, sl=sl: e.dma_start(
                out=xslots[sl],
                in_=xT[cc * 128:(cc + 1) * 128, hf * XP:(hf + 1) * XP]),
                dX[sl])
            sch.wait("scalar", dX[sl], dX[sl].n)
            sch.step("scalar", lambda e, cc=cc, hf=hf, a=xslots[sl]: e.activation(
                out=xbf[:, cc * S + hf * XP: cc * S + (hf + 1) * XP],
                in_=a, func=AF.Identity), sP)
            xstg_free_ms[sl] = sP.n
        xw_ms.append(sP.n)

    stage_x_window(0)

    sch.wait("vector", dSM, d_small)
    sch.step("vector", lambda e: e.tensor_scalar(
        out=bqs[:], in0=bqs[:], scalar1=0.125, scalar2=None, op0=ALU.mult), vP)
    sch.step("vector", lambda e: e.memset(ones1[:], 1.0), vP)
    sch.step("vector", lambda e: e.memset(ones64[:], 1.0), vP)
    sch.step("vector", lambda e: e.memset(ones2[0:64, 0:1], 1.0), vP)
    sch.step("vector", lambda e: e.memset(ones2[64:128, 0:1], 0.0), vP)
    sch.step("vector", lambda e: e.memset(ones2[0:64, 1:2], 0.0), vP)
    sch.step("vector", lambda e: e.memset(ones2[64:128, 1:2], 1.0), vP)
    vpad4 = vpad[:].rearrange("p (s k d) -> p s k d", s=8, k=2 * KT, d=65)
    sch.step("vector", lambda e: e.memset(vpad4[:, :, :, 64:65], 1.0), vP)
    sch.step("vector", lambda e: e.memset(nrm[:], 0.0), vP)
    prep_ms = vP.n

    wstg_free_ms = [0] * NWS

    def load_weight(src_ap, wbuf, stg=None, W=512, tguard=0, sync_v=0,
                    sync_t=0, qeng="gpsimd", chain=None, free_ms=None):
        if stg is None:
            stg = wslots
        if chain is None:
            chain = dW
        if free_ms is None:
            free_ms = wstg_free_ms
        nsl = len(stg)
        NH = D // W
        sch.wait("scalar", tP, tguard)
        if sync_v:
            sch.wait(qeng, vP, sync_v)
        if sync_t:
            sch.wait(qeng, tP, sync_t)
        for j in range(CH * NH):
            cc, h = j // NH, j % NH
            sl = j % nsl
            sch.wait(qeng, sP, free_ms[sl])
            sch.dma(qeng, lambda e, cc=cc, h=h, a=stg[sl]: e.dma_start(
                out=a,
                in_=src_ap[cc * 128:(cc + 1) * 128, h * W:(h + 1) * W]),
                chain[sl])
            sch.wait("scalar", chain[sl], chain[sl].n)
            sch.step("scalar", lambda e, cc=cc, h=h, a=stg[sl], w=wbuf:
                     e.activation(
                         out=w[:, cc * D + h * W: cc * D + (h + 1) * W],
                         in_=a, func=AF.Identity), sP)
            free_ms[sl] = sP.n
        return sP.n

    wv_ms = load_weight(wvT, wcur)
    for _hf in range(1, NXP):
        stage_x_window(_hf)

    pj = [pj0, pj1]
    pj_free_ms = [("v", 0), ("v", 0)]
    pj_idx = [0]

    def proj_group(wbuf, oc, mov_fn, wait_v=0, wait_s=0, wait_g=0, wait_g2=0):
        i = pj_idx[0]
        pj_idx[0] += 1
        slot = i % 2
        kind, ms = pj_free_ms[slot]
        sch.wait("tensor", vP if kind == "v" else sP, ms)
        if wait_v:
            sch.wait("tensor", vP, wait_v)
        if wait_s:
            sch.wait("tensor", sP, wait_s)
        if wait_g:
            sch.wait("tensor", dG, wait_g)
        if wait_g2:
            sch.wait("tensor", gC, wait_g2)
        for cc in range(CH):
            fn = lambda e, cc=cc, oc=oc, w=wbuf, bank=pj[slot]: e.matmul(
                bank[:, 0:512],
                w[:, cc * D + oc * 128: cc * D + (oc + 1) * 128],
                mov_fn(cc),
                start=(cc == 0), stop=(cc == CH - 1))
            if cc == CH - 1:
                sch.step("tensor", fn, tP)
            else:
                sch.add("tensor", fn)
        return slot, tP.n


    KC = 2 * KT
    for kap in range(KC):
        slot = kap % 2
        kind, ms = pj_free_ms[slot]
        sch.wait("tensor", vP if kind == "v" else sP, ms)
        sch.wait("tensor", sP, max(wv_ms, xw_ms[min(kap // 8, NXP - 1)]))
        for cc in range(CH):
            for dh in range(2):
                st = (cc == 0)
                last = (cc == CH - 1 and dh == 1)
                if dh == 0:
                    out_ap = pj[slot][0:64, 0:512]
                else:
                    out_ap = psS0[64:128, slot * 512:(slot + 1) * 512]
                fn = lambda e, cc=cc, dh=dh, kap=kap, st=st, o=out_ap: e.matmul(
                    o,
                    xbf[:, cc * S + kap * 64: cc * S + (kap + 1) * 64],
                    wcur[:, cc * D + dh * 512: cc * D + (dh + 1) * 512],
                    start=st, stop=(cc == CH - 1),
                    tile_position=(0, dh * 64), skip_group_check=True)
                if last:
                    sch.step("tensor", fn, tP)
                else:
                    sch.add("tensor", fn)
        sch.wait("vector", tP, tP.n)

        def ev(e, kap=kap, slot=slot, btop=pj[slot]):
            v4 = vpad[:].rearrange("p (s k d) -> p s k d", s=8, k=KC, d=65)
            out_top = v4[0:64, :, kap, 0:64]
            out_bot = v4[64:128, :, kap, 0:64]
            in_top = btop[0:64, 0:512].rearrange("p (s d) -> p s d", s=8, d=64)
            in_bot = psS0[64:128, slot * 512:(slot + 1) * 512].rearrange(
                "p (s d) -> p s d", s=8, d=64)
            bv_top = bvb[0:64, 0:512].rearrange("p (s d) -> p s d", s=8, d=64)
            bv_bot = bvb[64:128, 512:1024].rearrange("p (s d) -> p s d", s=8, d=64)
            e.tensor_tensor(out=out_top, in0=in_top, in1=bv_top, op=ALU.add)
            return e.tensor_tensor(out=out_bot, in0=in_bot, in1=bv_bot, op=ALU.add)
        sch.step("vector", ev, vP, k=1)
        pj_free_ms[slot] = ("v", vP.n)
    v_done_v = vP.n
    v_proj_t = tP.n


    wq_ms = load_weight(wqT, wcur, tguard=v_proj_t)
    wk_ms = load_weight(wkT, wnxt)

    sch.wait("scalar", vP, prep_ms)
    sch.wait("scalar", dSM, d_small)
    qk_chunk_s = []
    for oc in range(CH):
        for qt in range(QT):
            slot, ms = proj_group(
                wcur, oc,
                lambda cc, qt=qt: xbf[:, cc * S + qt * 512: cc * S + (qt + 1) * 512],
                wait_s=max(xw_ms[qt], wq_ms))
            sch.wait("scalar", tP, ms)
            sch.step("scalar", lambda e, oc=oc, qt=qt, b=pj[slot]: e.activation(
                out=qbf[:, oc * S + qt * 512: oc * S + (qt + 1) * 512],
                in_=b[:, 0:512], func=AF.Identity,
                bias=bqs[:, oc:oc + 1], scale=0.125), sP)
            pj_free_ms[slot] = ("s", sP.n)
        for qt in range(QT):
            slot, ms = proj_group(
                wnxt, oc,
                lambda cc, qt=qt: xbf[:, cc * S + qt * 512: cc * S + (qt + 1) * 512],
                wait_s=max(xw_ms[qt], wk_ms))
            sch.wait("scalar", tP, ms)
            sch.step("scalar", lambda e, oc=oc, qt=qt, b=pj[slot]: e.activation(
                out=kbf[:, oc * S + qt * 512: oc * S + (qt + 1) * 512],
                in_=b[:, 0:512], func=AF.Identity,
                bias=bks[:, oc:oc + 1], scale=1.0), sP)
            pj_free_ms[slot] = ("s", sP.n)
        qk_chunk_s.append(sP.n)
    q_done_s = sP.n
    k_done_s = sP.n
    q_proj_t = tP.n
    k_proj_t = tP.n

    wo_ms = load_weight(woT, wnxt, stg=wslots[:5], chain=dW[:5],
                        tguard=k_proj_t)
    ph_pj.close()
    pu2 = psum(ph_ab, "pu2", [128, 512], side="right")
    pu3 = psum(ph_ab, "pu3", [128, 512], side="right")

    sch.step("vector", lambda e: e.memset(negc[:], NEG_SHIFT), vP)
    sch.step("vector", lambda e: e.memset(magicb[:], float(MAGIC)), vP)
    negc_v_ms = vP.n

    psS = [psS0, psS1]
    psS_free_s = [0, 0]
    pu = [pu0, pu1, pu2, pu3]
    pu_free_v = [0, 0, 0, 0]
    ust = [ust0, ust1, ust2, ust3]
    ust_free = [0, 0, 0, 0]
    EXSL = 4
    ex_slot_free_t = [0] * EXSL
    rrow = [rrow0, rrow1, rrow2, rrow3]
    rbt = [rbt0, rbt1, rbt2, rbt3]
    rrow_free_g = [0] * 4
    rbt_free_v = [0] * 4
    unst_free_g = [0]
    first_exp = [True]
    exp_n = [0]
    sqb_guard_t = [0]
    uev = [0]

    def stage_scores_k(g, qt, kap):
        pg = kap % 2
        sch.wait("tensor", sP, psS_free_s[pg])
        if kap == 0:
            sch.wait("tensor", sP, qk_chunk_s[2 * g + 1])
        if first_exp[0] and kap == 0:
            sch.wait("tensor", vP, v_done_v)
        layout = (
            ((0, 0), 2 * g, 0, 0, 0),
            ((64, 64), 2 * g, 64, 0, 1),
            ((64, 0), 2 * g + 1, 64, 1, 0),
            ((0, 64), 2 * g + 1, 0, 1, 1),
        )
        for i, (tpos, ch, base, colh, rowh) in enumerate(layout):
            fn = lambda e, qt=qt, kap=kap, pg=pg, tpos=tpos, ch=ch, base=base, \
                colh=colh, rowh=rowh: e.matmul(
                psS[pg][rowh * 64: rowh * 64 + 64,
                        colh * 512:(colh + 1) * 512],
                kbf[base:base + 64,
                    ch * S + kap * 64: ch * S + (kap + 1) * 64],
                qbf[base:base + 64,
                    ch * S + qt * 512: ch * S + (qt + 1) * 512],
                start=True, stop=True, tile_position=tpos,
                skip_group_check=True)
            if i == 3:
                sch.step("tensor", fn, tP)
            else:
                sch.add("tensor", fn)
        sl = kap % EXSL
        if first_exp[0]:
            sch.wait("scalar", vP, negc_v_ms)
            first_exp[0] = False
        if exp_n[0] < EXSL:
            sch.wait("scalar", sP, wo_ms)
        exp_n[0] += 1
        sch.wait("scalar", tP, max(tP.n, ex_slot_free_t[sl]))
        sch.step("scalar", lambda e, sl=sl, pg=pg: e.activation(
            out=expt[:, sl * 1024:(sl + 1) * 1024],
            in_=psS[pg][:, 0:1024], func=AF.Exp,
            bias=negc[:], scale=1.0), sP, selfwait=False)
        psS_free_s[pg] = sP.n
        return sP.n

    def stage_av_k(g, kap, exp_ms):
        sl = kap % EXSL
        sch.wait("tensor", sP, exp_ms)
        work = (
            (0, 0, 2 * g, 0),
            (1, 64, 2 * g, 0),
            (2, 0, 2 * g + 1, 1),
            (3, 64, 2 * g + 1, 1),
        )
        for i, (bank, rb, scol, colh) in enumerate(work):
            fn = lambda e, kap=kap, sl=sl, bank=bank, rb=rb, scol=scol, \
                colh=colh: e.matmul(
                pu[bank][0:65, 0:512],
                vpad[rb:rb + 64,
                     scol * 65 * KC + kap * 65: scol * 65 * KC + (kap + 1) * 65],
                expt[rb:rb + 64,
                     sl * 1024 + colh * 512: sl * 1024 + (colh + 1) * 512],
                start=(kap == 0), stop=(kap == KC - 1),
                tile_position=(rb, 0), skip_group_check=True)
            if i == 3:
                sch.step("tensor", fn, tP)
            else:
                sch.add("tensor", fn)
        ex_slot_free_t[sl] = tP.n
        return tP.n

    def stage_u_copy(h, qt, bank, av_t):
        iu = uev[0] % 4
        uev[0] += 1
        sch.wait("vector", tP, av_t)
        if uev[0] <= 4:
            sch.wait("vector", sP, wo_ms)
        sch.wait("vector", vP, ust_free[iu])
        sch.wait("vector", dG, rrow_free_g[iu])
        sch.step("vector", lambda e, iu=iu, b=pu[bank]: e.tensor_copy(
            out=ust[iu][0:65, 0:512], in_=b[0:65, 0:512]), vP)
        pu_free_v[bank] = vP.n
        return (h, qt, iu)

    def stage_u_recip(info, fast=False):
        h, qt, iu = info
        if fast:
            sch.wait("scalar", vP, vP.n)
            sch.step("scalar", lambda e, iu=iu: e.activation(
                out=rrow[iu][0:1, 0:512], in_=ust[iu][64:65, 0:512],
                func=AF.Ln), sP)
            sch.step("scalar", lambda e, iu=iu: e.activation(
                out=sqb[0:1, iu * 512:(iu + 1) * 512], in_=rrow[iu][0:1, 0:512],
                func=AF.Exp, scale=-1.0), sP)
            sch.wait("tensor", sP, sP.n)
            sch.wait("tensor", vP, pu_free_v[iu])
            sch.step("tensor", lambda e, iu=iu: e.matmul(
                pu[iu][0:64, 0:512], ones64[0:1, 0:64],
                sqb[0:1, iu * 512:(iu + 1) * 512],
                start=True, stop=True), tP)
            sqb_guard_t[0] = tP.n
            return ("t", tP.n)
        sch.step("vector", lambda e, iu=iu: e.reciprocal(
            out=rrow[iu][0:1, 0:512], in_=ust[iu][64:65, 0:512]), vP)
        rms = vP.n
        sch.wait("gpsimd", vP, rms)
        sch.dma("gpsimd", lambda e, h=h, qt=qt, iu=iu: e.dma_start(
            out=scr_r[h * QT + qt: h * QT + qt + 1, :],
            in_=rrow[iu][0:1, 0:512]), dG)
        rrow_free_g[iu] = dG.n
        sch.wait("gpsimd", dG, dG.n)
        sch.wait("gpsimd", vP, rbt_free_v[iu])
        sch.dma("gpsimd", lambda e, h=h, qt=qt, iu=iu: e.dma_start(
            out=rbt[iu][0:64, 0:512],
            in_=scr_r[h * QT + qt: h * QT + qt + 1, :].broadcast_to([64, 512])),
            dG)
        return ("g", dG.n)

    def stage_u_norm(info, bc_ms):
        h, qt, iu = info
        ch = h // 2
        kind, ms = bc_ms
        if kind == "t":
            sch.wait("vector", tP, ms)
            rb = pu[iu]
        else:
            sch.wait("vector", dG, ms)
            rb = rbt[iu]
        if h % 2 == 0:
            sch.step("vector", lambda e, iu=iu, ch=ch, qt=qt, rb=rb:
                     e.tensor_tensor(
                         out=ubf[0:64, ch * S + qt * 512: ch * S + (qt + 1) * 512],
                         in0=ust[iu][0:64, 0:512], in1=rb[0:64, 0:512],
                         op=ALU.mult), vP)
        else:
            sch.wait("vector", dG, unst_free_g[0])
            sch.step("vector", lambda e, iu=iu, rb=rb: e.tensor_tensor(
                out=unst[0:64, 0:512],
                in0=ust[iu][0:64, 0:512], in1=rb[0:64, 0:512],
                op=ALU.mult), vP)
        rbt_free_v[iu] = vP.n
        ust_free[iu] = vP.n
        if h % 2 == 1:
            sch.wait("gpsimd", vP, vP.n)
            sch.dma("gpsimd", lambda e, ch=ch, qt=qt: e.dma_start(
                out=ubf[64:128, ch * S + qt * 512: ch * S + (qt + 1) * 512],
                in_=unst[0:64, 0:512]), dG)
            unst_free_g[0] = dG.n

    def stage_u_norms(infos, fast=False):
        bcs = [stage_u_recip(i, fast) for i in infos]
        for i, bc in zip(infos, bcs):
            stage_u_norm(i, bc)

    def stage_u_phase(g, qt, exp_last, fast=False):
        stage_av_k(g, KC - 1, exp_last)
        av_t = tP.n
        infos = [stage_u_copy(4 * g, qt, 0, av_t),
                 stage_u_copy(4 * g + 1, qt, 1, av_t),
                 stage_u_copy(4 * g + 3, qt, 2, av_t),
                 stage_u_copy(4 * g + 2, qt, 3, av_t)]
        stage_u_norms(infos, fast)

    sch.wait("gpsimd", vP, prep_ms)
    sch.dma("gpsimd", lambda e: e.dma_start(out=cc_in2, in_=nrm[0:1, 0:1]), dG)
    sch.wait("gpsimd", dG, dG.n)
    sch.step("gpsimd", lambda e: e.collective_compute(
        "AllReduce", ALU.max, replica_groups=[list(range(NC_))],
        ins=[cc_in2], outs=[cc_out2]), cC)

    u_qt_v = [0] * QT
    u_qt_g = [0] * QT
    prev = None
    for g in range(H // 4):
        for qt in range(QT):
            exp_ms = []
            for kap in range(KC):
                exp_ms.append(stage_scores_k(g, qt, kap))
                if kap == 0 and prev is not None:
                    stage_u_phase(*prev)
                    if prev[0] == H // 4 - 1:
                        u_qt_v[prev[1]] = vP.n
                        u_qt_g[prev[1]] = dG.n
                if kap == 2:
                    sch.wait("tensor", vP, max(pu_free_v))
                    stage_av_k(g, 0, exp_ms[0])
                    stage_av_k(g, 1, exp_ms[1])
                elif kap >= 3:
                    stage_av_k(g, kap - 1, exp_ms[kap - 1])
            prev = (g, qt, exp_ms[-1])
    stage_u_phase(*prev, fast=True)
    u_qt_v[prev[1]] = vP.n
    u_qt_g[prev[1]] = dG.n
    u_done_v = vP.n
    u_done_g = dG.n

    ph_ab.close()

    x1 = sb(root, "x1", [128, CH * S], F32)
    rowst = sb(root, "rowst", [1, 1024], F32)
    rtmp = sb(root, "rtmp", [1, 512], F32)
    rscr = sb(root, "rscr", [1, 512], F32)
    ztmp0 = sb(root, "ztmp0", [128, 512], F32)
    ztmp1 = sb(root, "ztmp1", [128, 512], F32)
    ztmp = [ztmp0, ztmp1]
    ztmp_free_v = [0, 0]
    mbeb = sb(root, "mbeb", [128, 1024], F32)
    xres0 = sb(ph_c, "xres0", [128, 512], F32)
    xres1 = sb(ph_c, "xres1", [128, 512], F32)
    xresg = [xres0, xres1]
    xres_free = [u_done_v, u_done_v]
    sch.wait("sync", dG, u_done_g)
    pjC0 = psum(root, "pjC0", [128, 512])
    pjC1 = psum(root, "pjC1", [128, 512])
    pj[0] = pjC0
    pj[1] = pjC1
    pj_free_ms[0] = ("v", 0)
    pj_free_ms[1] = ("v", 0)
    paA = psum(root, "paA", [128, 512])
    paB = psum(root, "paB", [128, 512])

    ri = [0]

    def stage_o_proj(qt):
        for cc in range(CH):
            k = ri[0] % 2
            ri[0] += 1
            sch.wait("sync", vP, xres_free[k])
            sch.dma("sync", lambda e, cc=cc, qt=qt, k=k: e.dma_start(
                out=xresg[k][:],
                in_=xT[cc * 128:(cc + 1) * 128, qt * 512:(qt + 1) * 512]), dX[k])
            x_rdy = dX[k].n
            slot, ms = proj_group(
                wnxt, cc,
                lambda ic, qt=qt: ubf[:, ic * S + qt * 512: ic * S + (qt + 1) * 512],
                wait_v=u_qt_v[qt], wait_s=wo_ms, wait_g=u_qt_g[qt])
            sch.wait("scalar", tP, ms)
            zi = pj_idx[0] % 2
            zt = ztmp[zi]
            sch.wait("scalar", vP, ztmp_free_v[zi])
            sch.step("scalar", lambda e, cc=cc, b=pj[slot], zt=zt: e.activation(
                out=zt[:], in_=b[:, 0:512], func=AF.Identity,
                bias=bos[:, cc:cc + 1]), sP)
            pj_free_ms[slot] = ("s", sP.n)
            sch.wait("vector", sP, sP.n)
            sch.wait("vector", dX[k], x_rdy)
            sch.step("vector", lambda e, cc=cc, qt=qt, zt=zt, k=k:
                     e.tensor_tensor(
                         out=x1[:, cc * S + qt * 512: cc * S + (qt + 1) * 512],
                         in0=zt[:], in1=xresg[k][:], op=ALU.add), vP)
            ztmp_free_v[zi] = vP.n
            xres_free[k] = vP.n
        return vP.n

    rowst_free_g = [0]

    def stage_ln_qt(buf, qt, ready_v, stream_out=False):
        sch.wait("scalar", vP, ready_v)
        sch.wait("scalar", tP, sqb_guard_t[0])
        stat_t = []
        for cc in range(CH):
            sl0 = cc * S + qt * 512
            sb0 = (cc % 2) * 1024
            if cc >= 2:
                sch.wait("scalar", tP, stat_t[cc - 2])
            sch.step("scalar", lambda e, sl0=sl0, sb0=sb0: e.activation(
                out=sqb[:, sb0:sb0 + 512], in_=buf[:, sl0:sl0 + 512],
                func=AF.Identity), sP)
            sch.step("scalar", lambda e, sl0=sl0, sb0=sb0: e.activation(
                out=sqb[:, sb0 + 512:sb0 + 1024], in_=buf[:, sl0:sl0 + 512],
                func=AF.Square), sP)
            sch.wait("tensor", sP, sP.n)
            for j in range(2):
                bank = paA if j == 0 else paB
                fn = lambda e, j=j, sb0=sb0, bank=bank, st=(cc == 0), \
                    sp=(cc == CH - 1): e.matmul(
                    bank[0:1, 0:512], ones1[:],
                    sqb[:, sb0 + j * 512: sb0 + (j + 1) * 512],
                    start=st, stop=sp)
                if j == 1:
                    sch.step("tensor", fn, tP)
                else:
                    sch.add("tensor", fn)
            stat_t.append(tP.n)
        sch.wait("vector", tP, tP.n)
        sch.wait("vector", dG, rowst_free_g[0])
        sch.step("vector", lambda e: e.tensor_scalar(
            out=rowst[0:1, 0:512], in0=paA[0:1, 0:512],
            scalar1=float(1.0 / D), scalar2=None, op0=ALU.mult), vP)
        sch.step("vector", lambda e: e.tensor_scalar(
            out=rowst[0:1, 512:1024], in0=paB[0:1, 0:512],
            scalar1=float(1.0 / D), scalar2=None, op0=ALU.mult), vP)
        sch.step("vector", lambda e: e.tensor_tensor(
            out=rtmp[0:1, :], in0=rowst[0:1, 0:512], in1=rowst[0:1, 0:512],
            op=ALU.mult), vP)
        sch.step("vector", lambda e: e.scalar_tensor_tensor(
            out=rowst[0:1, 512:1024], in0=rowst[0:1, 512:1024], scalar=1e-5,
            in1=rtmp[0:1, :], op0=ALU.add, op1=ALU.subtract), vP)
        sch.wait("scalar", vP, vP.n)
        sch.wait("scalar", dG, rowst_free_g[0])
        sch.step("scalar", lambda e: e.activation(
            out=rowst[0:1, 512:1024], in_=rowst[0:1, 512:1024],
            func=AF.Ln), sP)
        sch.step("scalar", lambda e: e.activation(
            out=rowst[0:1, 512:1024], in_=rowst[0:1, 512:1024],
            func=AF.Exp, scale=-0.5), sP)
        sch.wait("vector", sP, sP.n)
        sch.wait("gpsimd", vP, vP.n)
        sch.wait("gpsimd", sP, sP.n)
        sch.dma("gpsimd", lambda e: e.dma_start(
            out=scr_rows, in_=rowst[0:1, :]), dG)
        rowst_free_g[0] = dG.n
        sch.wait("gpsimd", dG, dG.n)
        sch.dma("gpsimd", lambda e: e.dma_start(
            out=mbeb[:],
            in_=scr_rows.rearrange("a b -> (a b)")
            .rearrange("(o f) -> o f", o=1).broadcast_to([128, 1024])), dG)
        sch.wait("vector", dG, dG.n)
        sch.wait("gpsimd", dG, dG.n)
        if not stream_out:
            ba = buf[:].rearrange("p (c s) -> p c s", c=CH, s=S)[
                :, :, qt * 512:(qt + 1) * 512]
            mb_s = mbeb[:, 0:512].unsqueeze(1).broadcast_to([128, CH, 512])
            rb_s = mbeb[:, 512:1024].unsqueeze(1).broadcast_to([128, CH, 512])
            sch.step("vector", lambda e, ba=ba, m=mb_s: e.tensor_tensor(
                out=ba, in0=ba, in1=m, op=ALU.subtract), vP)
            sch.step("vector", lambda e, ba=ba, m=rb_s: e.tensor_tensor(
                out=ba, in0=ba, in1=m, op=ALU.mult), vP)
            return (vP.n, gC.n)
        sch.wait("sync", gC, gC.n)
        sch.wait("gpsimd", gC, gC.n)
        for cc in range(CH):
            sl0 = cc * S + qt * 512
            sch.step("vector", lambda e, sl0=sl0: e.tensor_tensor(
                out=buf[:, sl0:sl0 + 512], in0=buf[:, sl0:sl0 + 512],
                in1=mbeb[:, 0:512], op=ALU.subtract), vP)
            sch.step("vector", lambda e, sl0=sl0: e.tensor_tensor(
                out=buf[:, sl0:sl0 + 512], in0=buf[:, sl0:sl0 + 512],
                in1=mbeb[:, 512:1024], op=ALU.mult), vP)
            eng, ctr = (("sync", dS) if cc % 2 == 0 else ("gpsimd", dG))
            sch.wait(eng, vP, vP.n)
            sch.dma(eng, lambda e, cc=cc, sl0=sl0, qt=qt: e.dma_start(
                out=outT[cc * 128:(cc + 1) * 128, qt * 512:(qt + 1) * 512],
                in_=buf[:, sl0:sl0 + 512]), ctr)
        return (vP.n, gC.n)

    def stage_absmax_qt(qt):
        for cc in range(CH):
            sch.step("vector", lambda e, cc=cc, qt=qt: e.tensor_reduce(
                out=red32[:, qt * CH + cc: qt * CH + cc + 1],
                in_=x1[:, cc * S + qt * 512: cc * S + (qt + 1) * 512], axis=AX,
                op=ALU.max, apply_absolute_value=True), vP)

    oq_ms = []
    for qt in range(QT):
        oq_ms.append(stage_o_proj(qt))
        if qt >= 1:
            stage_ln_qt(x1, qt - 1, oq_ms[qt - 1])
            stage_absmax_qt(qt - 1)
    o_proj_t = tP.n
    xres_done_v = vP.n
    ph_c.close()
    wstgi0 = sb(root, "wstgi0", [128, D], I8)
    wstgi1 = sb(root, "wstgi1", [128, D], I8)
    ln1_v, ln1_g = stage_ln_qt(x1, QT - 1, oq_ms[QT - 1])
    stage_absmax_qt(QT - 1)
    wf_ms = load_weight(wfT, wcur, [wstgi0[:], wstgi1[:]], W=1024,
                        tguard=q_proj_t, sync_v=xres_done_v, sync_t=o_proj_t,
                        qeng="sync", chain=dF, free_ms=[0, 0])
    sch.wait("vector", gC, ln1_g)
    ln1_ms = vP.n

    sch.step("vector", lambda e: e.tensor_reduce(
        out=gmaxb[:], in_=red32[:], axis=AX, op=ALU.max,
        apply_absolute_value=False), vP)
    sch.wait("gpsimd", vP, vP.n)
    sch.dma("gpsimd", lambda e: e.dma_start(out=scr_red, in_=gmaxb[:]), dG)
    sch.wait("gpsimd", dG, dG.n)
    sch.dma("gpsimd", lambda e: e.dma_start(out=nrow[:], in_=scr_red), dG)
    sch.wait("vector", dG, dG.n)
    sch.step("vector", lambda e: e.tensor_reduce(
        out=sc1[0:1, 0:1], in_=nrow[:], axis=AX, op=ALU.max,
        apply_absolute_value=False), vP)
    sch.wait("gpsimd", vP, vP.n)
    sch.dma("gpsimd", lambda e: e.dma_start(out=cc_in, in_=sc1[0:1, 0:1]), dG)
    sch.wait("gpsimd", dG, dG.n)
    sch.step("gpsimd", lambda e: e.collective_compute(
        "AllReduce", ALU.max, replica_groups=[list(range(NC_))],
        ins=[cc_in], outs=[cc_out]), cC)
    sch.wait("gpsimd", cC, 2)
    sch.dma("gpsimd", lambda e: e.dma_start(
        out=gmaxb[:], in_=cc_out.broadcast_to([128, 1])), dG)
    sch.wait("vector", dG, dG.n)
    sch.step("vector", lambda e: e.tensor_scalar(
        out=xsb[:], in0=gmaxb[:], scalar1=float(1.0 / 127.0), scalar2=1e-5,
        op0=ALU.mult, op1=ALU.max), vP)
    sch.step("vector", lambda e: e.reciprocal(out=invxs[:], in_=xsb[:]), vP)
    sch.step("vector", lambda e: e.tensor_tensor(
        out=scs[:], in0=wss[:], in1=xsb[:].broadcast_to([128, CH]),
        op=ALU.mult), vP)
    xs_ms = vP.n

    ph_wo.close()
    xq = actbf
    y2 = sb(root, "y2", [128, CH * S], F32)
    q32a = sb(root, "q32a", [128, 512], F32)
    q32a2 = sb(root, "q32a2", [128, 512], F32)
    q32ab = [q32a, q32a2]
    q32_free = [0, 0]

    def stage_quant_cc(qt, cc):
        sl0 = cc * S + qt * 512
        if True:
            sch.wait("scalar", vP, q32_free[cc % 2])
            sch.step("scalar", lambda e, sl0=sl0, cc=cc: e.activation(
                out=q32ab[cc % 2][:], in_=x1[:, sl0:sl0 + 512],
                func=AF.Identity, scale=invxs[:],
                bias=magicb[:]), sP)
            ms1 = sP.n
            sch.wait("vector", sP, ms1)
            sch.step("vector", lambda e, cc=cc: e.tensor_scalar(
                out=q32ab[cc % 2][:], in0=q32ab[cc % 2][:], scalar1=float(MAGIC),
                scalar2=127.0, op0=ALU.subtract, op1=ALU.min), vP)
            sch.step("vector", lambda e, cc=cc, sl0=sl0: e.tensor_scalar(
                out=xq[:, sl0:sl0 + 512], in0=q32ab[cc % 2][:], scalar1=-128.0,
                scalar2=None, op0=ALU.max), vP)
            q32_free[cc % 2] = vP.n
            sch.step("scalar", lambda e, cc=cc, sl0=sl0: e.activation(
                out=x1[:, sl0:sl0 + 512], in_=x1[:, sl0:sl0 + 512],
                func=AF.Identity, scale=1.0, bias=fbs[:, cc:cc + 1]), sP)

    def stage_quant(qt):
        sch.wait("vector", vP, xs_ms)
        sch.wait("vector", tP, o_proj_t)
        sch.wait("scalar", vP, xs_ms)
        sch.wait("scalar", tP, o_proj_t)
        for cc in range(CH):
            stage_quant_cc(qt, cc)
        return (vP.n, gC.n)

    def stage_ffn(qt, q_ms):
        for cc in range(CH):
            slot, ms = proj_group(
                wcur, cc,
                lambda ic, qt=qt: xq[:, ic * S + qt * 512: ic * S + (qt + 1) * 512],
                wait_v=q_ms[0], wait_s=wf_ms, wait_g2=q_ms[1])
            sch.wait("scalar", tP, ms)
            zi = pj_idx[0] % 2
            zt = ztmp[zi]
            sch.wait("scalar", vP, ztmp_free_v[zi])
            sch.step("scalar", lambda e, cc=cc, b=pj[slot], zt=zt: e.activation(
                out=zt[:], in_=b[:, 0:512], func=AF.Identity,
                scale=scs[:, cc:cc + 1]), sP)
            pj_free_ms[slot] = ("s", sP.n)
            sch.wait("vector", sP, sP.n)
            sch.step("vector", lambda e, cc=cc, qt=qt, zt=zt:
                     e.tensor_tensor(
                         out=y2[:, cc * S + qt * 512: cc * S + (qt + 1) * 512],
                         in0=zt[:],
                         in1=x1[:, cc * S + qt * 512: cc * S + (qt + 1) * 512],
                         op=ALU.add), vP)
            ztmp_free_v[zi] = vP.n
            if qt + 1 < QT:
                stage_quant_cc(qt + 1, cc)
        return vP.n, (vP.n, gC.n)

    ln2_qt_ms = [0] * QT
    ffn_ms = [0] * QT
    q_next = stage_quant(0)
    for qt in range(QT):
        ffn_ms[qt], q_next = stage_ffn(qt, q_next)
        if qt >= 1:
            stream = (qt == QT - 1)
            ln2_qt_ms[qt - 1] = stage_ln_qt(y2, qt - 1, ffn_ms[qt - 1],
                                            stream_out=stream)
            if not stream:
                for cc in range(CH):
                    sch.wait("sync", vP, ln2_qt_ms[qt - 1][0])
                    sch.wait("sync", gC, ln2_qt_ms[qt - 1][1])
                    sch.dma("sync", lambda e, cc=cc, qt=qt: e.dma_start(
                        out=outT[cc * 128:(cc + 1) * 128,
                                 (qt - 1) * 512: qt * 512],
                        in_=y2[:, cc * S + (qt - 1) * 512: cc * S + qt * 512]),
                        dS)
    ln2_qt_ms[QT - 1] = stage_ln_qt(y2, QT - 1, ffn_ms[QT - 1],
                                    stream_out=True)

    if debug:
        sch.wait("sync", vP, vP.n)
        sch.wait("sync", sP, sP.n)
        sch.wait("sync", dG, dG.n)
        for nm, buf in (("dX1", x1), ("dXQ", xq), ("dXS", xsb)):
            sch.dma("sync", lambda e, nm=nm, b=buf: e.dma_start(
                out=dbg[nm], in_=b[:]), dS)

    sch.wait("sync", dS, dS.n)
    sch.wait("sync", dG, dG.n)

    with nc.Block() as block:
        @block.sync
        def _(sync):
            for fn in sch.q["sync"]:
                fn(sync)

        @block.vector
        def _(vector):
            for fn in sch.q["vector"]:
                fn(vector)

        @block.scalar
        def _(scalar):
            for fn in sch.q["scalar"]:
                fn(scalar)

        @block.tensor
        def _(tensor):
            for fn in sch.q["tensor"]:
                fn(tensor)

        @block.gpsimd
        def _(gpsimd):
            for fn in sch.q["gpsimd"]:
                fn(gpsimd)

    root.close()
    return nc


_NC_CACHE = {}


def _get_nc(debug=False):
    if debug not in _NC_CACHE:
        _NC_CACHE[debug] = build_nc(debug)
    return _NC_CACHE[debug]


def make_in_maps(L0_attn_x, L0_attn_Wq, L0_attn_bq, L0_attn_Wk, L0_attn_bk,
                 L0_attn_Wv, L0_attn_bv, L0_attn_Wo, L0_attn_bo,
                 L0_ffn_weight_int8, L0_ffn_weight_scale, L0_ffn_bias):
    x = np.asarray(L0_attn_x, dtype=np.float32)
    shared = {
        "wqT": np.ascontiguousarray(np.asarray(L0_attn_Wq, np.float32).T),
        "wkT": np.ascontiguousarray(np.asarray(L0_attn_Wk, np.float32).T),
        "wvT": np.ascontiguousarray(
            np.asarray(L0_attn_Wv, np.float32).T.reshape(D, 16, 64)
            [:, [0, 3, 4, 7, 8, 11, 12, 15, 1, 2, 5, 6, 9, 10, 13, 14], :]
            .reshape(D, D)),
        "woT": np.ascontiguousarray(np.asarray(L0_attn_Wo, np.float32).T),
        "wfT": np.ascontiguousarray(np.asarray(L0_ffn_weight_int8, np.int8).T),
        "bq": np.ascontiguousarray(np.asarray(L0_attn_bq, np.float32)),
        "bk": np.ascontiguousarray(np.asarray(L0_attn_bk, np.float32)),
        "bv": np.ascontiguousarray(
            np.asarray(L0_attn_bv, np.float32).reshape(16, 64)
            [[0, 3, 4, 7, 8, 11, 12, 15, 1, 2, 5, 6, 9, 10, 13, 14], :]
            .reshape(D)),
        "bo": np.ascontiguousarray(np.asarray(L0_attn_bo, np.float32)),
        "fb": np.ascontiguousarray(np.asarray(L0_ffn_bias, np.float32)),
        "ws": np.ascontiguousarray(
            np.asarray(L0_ffn_weight_scale, np.float32).reshape(-1)),
    }
    in_maps = []
    for b in range(B):
        m = dict(shared)
        m["xT"] = np.ascontiguousarray(x[b].T)
        in_maps.append(m)
    return in_maps


def kernel(**inputs):
    nc = _get_nc(debug=False)
    in_maps = make_in_maps(**inputs)
    res = run_bass_kernel_spmd(nc, in_maps, core_ids=list(range(NC_)))
    out = np.empty((B, S, D), dtype=np.float32)
    for b in range(B):
        out[b] = res.results[b]["outT"].T
    return out
